# version 01
# baseline (speedup 1.0000x reference)
import numpy as np
import ml_dtypes

import concourse.bass as bass
import concourse.mybir as mybir
from concourse.bass_utils import run_bass_kernel_spmd

F32 = mybir.dt.float32
BF16 = mybir.dt.bfloat16
AF = mybir.ActivationFunctionType
ALU = mybir.AluOpType
AXX = mybir.AxisListType.X

T = 2048
D = 1024
NT = 16
ND = 8
DH = 64
DEPTH = 6
DI = 2730
NJ = 22

ENGS = ("tensor", "vector", "scalar", "sync", "gpsimd")


def host_consts():
    t = np.arange(T, dtype=np.float64)
    half = np.arange(0, DH, 2, dtype=np.float64)
    inv_freq = 1.0 / 10000.0 ** (half / DH)
    f = t[:, None] * inv_freq[None, :]
    freqs = np.concatenate([f, f], -1)
    base_scale = (half + 0.4 * DH) / (1.4 * DH)
    power = (t - T // 2) / (256 // 2)
    scale = base_scale[None, :] ** power[:, None]
    scale = np.concatenate([scale, scale], -1)
    cos, sin = np.cos(freqs), np.sin(freqs)
    sgn = np.where(np.arange(DH) < DH // 2, -1.0, 1.0)[None, :]
    rotA = (8.0 * cos * scale).astype(np.float32)
    rotB = (8.0 * sin * scale * sgn).astype(np.float32)
    rotC = (cos / scale).astype(np.float32)
    rotD = (sin / scale * sgn).astype(np.float32)
    r = np.arange(128)[:, None]
    j = np.arange(384)[None, :]
    mask01 = ((j >= r) & (j <= r + 256)).astype(ml_dtypes.bfloat16)
    ident = np.eye(128, dtype=ml_dtypes.bfloat16)
    onesc = np.ones((1, 128), dtype=ml_dtypes.bfloat16)
    return dict(rotA=rotA, rotB=rotB, rotC=rotC, rotD=rotD,
                mask01=mask01, ident=ident, onesc=onesc)


class Res:
    __slots__ = ("w", "r", "name")

    def __init__(self, name="", carry=()):
        self.name = name
        self.w = None
        self.r = list(carry)


def retire(*reses):
    hs = []
    for res in reses:
        if res is None:
            continue
        if res.w is not None:
            hs.append(res.w)
        hs.extend(res.r)
    return hs


class Em:

    def __init__(self):
        self.ops = {e: [] for e in ENGS}
        self.last_w = {e: set() for e in ENGS}
        self.junk_fn = None
        self.njunk = 0

    def do(self, eng, fn, r=(), w=(), dma=False):
        if (eng == "vector" and not dma and self.junk_fn is not None
                and self.last_w["vector"] and any(res in self.last_w["vector"] for res in r)):
            self.njunk += 1
            jf = self.junk_fn
            idxj = len(self.ops["vector"])
            self.ops["vector"].append([jf, set(), "c", False])
            self.last_w["vector"] = set()
        deps = set()
        for res in r:
            if res.w is not None:
                deps.add(res.w)
        for res in w:
            if res.w is not None:
                deps.add(res.w)
            deps.update(res.r)
        kind = "d" if dma else "c"
        idx = len(self.ops[eng])
        h = (eng, kind, idx)
        self.ops[eng].append([fn, deps, kind, False])
        if not dma:
            self.last_w[eng] = set(w)
        for res in r:
            res.r.append(h)
        for res in w:
            res.w = h
            res.r = []
        return h

    def finalize(self, nc, fence_handles):
        for eng in ENGS:
            for op in self.ops[eng]:
                for (deng, dkind, didx) in op[1]:
                    if deng == eng and dkind == "c":
                        continue
                    self.ops[deng][didx][3] = True
        for (deng, dkind, didx) in fence_handles:
            self.ops[deng][didx][3] = True
        pre = {}
        for eng in ENGS:
            cc = dc = 0
            arr = []
            for op in self.ops[eng]:
                if op[3]:
                    if op[2] == "d":
                        dc += 1
                    else:
                        cc += 1
                arr.append((cc, dc))
            pre[eng] = arr

        import contextlib
        ctx = contextlib.ExitStack()
        sems = {}
        for eng in ENGS:
            if any(op[3] and op[2] == "c" for op in self.ops[eng]):
                sems[(eng, "c")] = ctx.enter_context(nc.semaphore(f"s_{eng}_c"))
            if any(op[3] and op[2] == "d" for op in self.ops[eng]):
                sems[(eng, "d")] = ctx.enter_context(nc.semaphore(f"s_{eng}_d"))

        def waitval(h):
            deng, dkind, didx = h
            cc, dc = pre[deng][didx]
            return (16 * dc) if dkind == "d" else cc

        blk = ctx.enter_context(nc.Block())
        em = self

        def mk_body(eng):
            def body(e):
                water = {}
                for i, (fn, deps, kind, inc) in enumerate(em.ops[eng]):
                    need = {}
                    for h in deps:
                        deng, dkind, _ = h
                        if deng == eng and dkind == "c":
                            continue
                        key = (deng, dkind)
                        v = waitval(h)
                        if v > need.get(key, 0):
                            need[key] = v
                    for key in sorted(need):
                        v = need[key]
                        if v > water.get(key, 0):
                            e.wait_ge(sems[key], v)
                            water[key] = v
                    inst = fn(e)
                    if inc:
                        if kind == "d":
                            inst.then_inc(sems[(eng, "d")], 16)
                        else:
                            inst.then_inc(sems[(eng, "c")], 1)
                if eng == "sync":
                    need = {}
                    for h in fence_handles:
                        key = (h[0], h[1])
                        v = waitval(h)
                        if v > need.get(key, 0):
                            need[key] = v
                    for key in sorted(need):
                        v = need[key]
                        if v > water.get(key, 0):
                            e.wait_ge(sems[key], v)
                            water[key] = v
            return body

        blk.tensor(mk_body("tensor"))
        blk.vector(mk_body("vector"))
        blk.scalar(mk_body("scalar"))
        blk.sync(mk_body("sync"))
        blk.gpsimd(mk_body("gpsimd"))
        ctx.close()


def AP(t, off, *dims):
    return bass.AP(t, off, [list(d) for d in dims])


def build_program(n_layers=DEPTH, dbg=None):
    nc = bass.Bass("TRN2", target_bir_lowering=False)

    def din(name, shape, dt=F32):
        return nc.dram_tensor(name, shape, dt, kind="ExternalInput")

    x0d = din("x0", [T, D])
    tdd = din("td", [D]); tfd = din("tf", [D])
    tmkd = din("tmk", [D]); tmvd = din("tmv", [D]); tmrd = din("tmr", [D])
    rW = {"k": din("rWk", [D, D]), "v": din("rWv", [D, D]),
          "r": din("rWr", [D, D]), "o": din("rWo", [D, D])}
    Wqkvd = din("Wqkv", [DEPTH, D, 3 * D])
    qscd = din("qsc", [DEPTH, DH]); kscd = din("ksc", [DEPTH, DH])
    iWod = din("iWo", [DEPTH, D, D])
    lngd = din("lng", [DEPTH, D]); lnbd = din("lnb", [DEPTH, D])
    W1d = din("W1", [DEPTH, D, 2 * DI]); b1d = din("b1", [DEPTH, 2 * DI])
    W2d = din("W2", [DEPTH, DI, D]); b2d = din("b2", [DEPTH, D])
    c_id = din("ident", [128, 128], BF16)
    c_mask = din("mask01", [128, 384], BF16)
    c_ones = din("onesc", [1, 128], BF16)
    c_rot = {k: din(k, [T, DH]) for k in ("rotA", "rotB", "rotC", "rotD")}
    outd = nc.dram_tensor("out", [T, D], F32, kind="ExternalOutput")
    x1pk = nc.dram_tensor("x1park", [T, D], F32, kind="Internal")
    dbg_t = {k: nc.dram_tensor(k, shp, F32, kind="ExternalOutput")
             for k, shp in (dbg or {}).items()}

    sb = nc.alloc_sbuf_tensor
    X = sb("X", [128, NT * 1024], F32)
    A = sb("Acm", [128, ND * 2052], BF16)

    RB_ = 75456
    roff, _ = nc.bump_sbuf(RB_)
    at_ = nc.alloc_sbuf_tensor_at
    Fb = [at_(f"F{i}", [128, 2052], F32, offset=roff + i * 8224) for i in range(5)]
    RWc = at_("RWcm", [128, ND * 2048], BF16, offset=roff + 41120)
    QC = at_("QC", [128, 8 * 1024], BF16, offset=roff)
    KC = at_("KC", [128, 8 * 1280], BF16, offset=roff + 16384)
    VC = at_("VC", [128, 10 * 1040], BF16, offset=roff + 36864)
    DT = {k: at_(f"DT{k}", [128, 1024], BF16, offset=roff + 57664 + i * 2048)
          for i, k in enumerate(("AQ", "BQ", "CK", "DK"))}
    BASE = at_("BASE", [128, 1024], F32, offset=roff + 65856)
    S0 = at_("S0", [128, 512], F32, offset=roff + 65856)
    S1 = at_("S1", [128, 512], F32, offset=roff + 67904)
    S2 = at_("S2", [128, 512], F32, offset=roff + 69952)
    EXM = at_("EXM", [128, 768], BF16, offset=roff + 72000)
    PTS = at_("PTS", [128, 768], BF16, offset=roff + 73536)
    PVo = at_("PVo", [128, 64], BF16, offset=roff + 75072)
    PVo2 = at_("PVo2", [128, 64], BF16, offset=roff + 75200)
    HC = at_("HC", [128, NJ * 1024], BF16, offset=roff)
    YB = at_("YB", [128, 1024], BF16, offset=roff + 45056)
    SQs = at_("SQs", [128, 1024], F32, offset=roff + 47104)
    ASb = at_("ASb", [128, 512], F32, offset=roff + 51200)
    GSb = at_("GSb", [128, 512], F32, offset=roff + 53248)
    YB0 = at_("YB0", [128, 1024], BF16, offset=roff)
    TS = [at_(f"TS{i}", [128, 1024], F32, offset=roff + 2048 + i * 4096) for i in range(4)]

    WB_ = 24576
    woff, _ = nc.bump_sbuf(WB_)
    WQS = [at_(f"WQS{i}", [128, 4096], BF16, offset=woff + i * 8192) for i in range(2)]
    W1S = [at_(f"W1S{i}", [128, 1024], BF16, offset=woff + 16384 + i * 2048) for i in range(4)]
    W2S = [at_(f"W2S{i}", [128, 1024], BF16, offset=woff + i * 2048) for i in range(2)]
    RKS = [at_(f"RKS{i}", [128, 6144], BF16, offset=woff + i * 12288) for i in range(2)]
    WOT = at_("WOT", [128, 8 * 1024], BF16, offset=woff)

    ident = sb("identS", [128, 128], BF16)
    maskS = sb("maskS", [128, 384], BF16)
    onesS = sb("onesS", [1, 128], BF16)
    WSTG = sb("WSTG", [128, 1024], BF16)
    ST = sb("ST", [128, 96], F32)
    SHP = sb("SHP", [128, 32], F32)
    MU = sb("MU", [128, 16], F32)
    RS = sb("RSb", [128, 16], F32)
    MB = sb("MBb", [128, 16], F32)
    RSp = sb("RSp", [128, 16], F32)
    MBp = sb("MBp", [128, 16], F32)
    SSQ = sb("SSQ", [128, 8], F32)
    CL = sb("CL", [128, 64], F32)
    QSB = sb("QSB", [128, 64], F32); QSS = sb("QSS", [128, 64], F32)
    KSB = sb("KSB", [128, 64], F32); KSS = sb("KSS", [128, 64], F32)
    B1A = sb("B1A", [128, 22], F32); B1G = sb("B1G", [128, 22], F32)
    B2R = sb("B2R", [1, 1024], BF16)
    NMX = sb("NMX", [128, 2], F32)
    CST = sb("CST", [128, 2], F32)
    RPC = sb("RPC", [128, 2], F32)
    MB1 = sb("MB1", [1, 16], BF16)

    PS = [nc.alloc_psum_tensor(f"ps{i}", [128, 1024], F32) for i in range(4)]

    def bank(i):
        return PS[i // 2], (i % 2) * 512

    JNK = sb("JNK", [128, 8], F32)
    em = Em()
    em.junk_fn = lambda e: e.tensor_scalar(JNK[:, 0:1], JNK[:, 0:1], 1.0, None, ALU.mult)
    rX = [Res(f"X{i}") for i in range(NT)]
    rA = [Res(f"A{i}") for i in range(ND)]
    rB = [Res(f"B{i}") for i in range(8)]
    rCL = Res("CL"); rST = Res("ST"); rMU = Res("MU"); rRS = Res("RS")
    rMB = Res("MB"); rRSp = Res("RSp"); rMBp = Res("MBp"); rSSQ = Res("SSQ")
    rSHP = Res("SHP"); rNMX = Res("NMX"); rRPC = Res("RPC")
    rQS = Res("QS"); rB1 = Res("B1"); rB2 = Res("B2"); rMB1 = Res("MB1")
    rMisc = Res("misc"); rWSTG = Res("wstg")
    store_h = []

    REG_R = []
    REG_W = []

    def rfence(reg):
        return retire(*reg)

    def rnew(reg, name, fence):
        res = Res(name, carry=fence)
        reg.append(res)
        return res

    def dma(eng, dst_ap, src_ap, r=(), w=(), slow=False):
        return em.do(eng, lambda e, dd=dst_ap, ss=src_ap, sl=slow: e.dma_start(
            out=dd, in_=ss, allow_slow_non_contiguous=sl),
            r=list(r), w=list(w), dma=True)

    def dbg_dump(name, src_ap, rsrc):
        if name in dbg_t:
            h = dma("gpsimd", dbg_t[name].ap(), src_ap, r=[rsrc])
            store_h.append(h)

    for tt in range(NT):
        dma("sync", X[:, 1024 * tt:1024 * (tt + 1)],
            AP(x0d, 1024 * 128 * tt, [1024, 128], [1, 1024]), w=[rX[tt]])
    dma("sync", ident[:], c_id.ap(), w=[rMisc])
    dma("sync", maskS[:], c_mask.ap(), w=[rMisc])
    dma("sync", onesS[:], c_ones.ap(), w=[rMisc])
    for dsto, srct in ((0, tmkd), (8, tmvd), (16, tmrd), (24, tdd), (32, tfd)):
        dma("sync", AP(CL, dsto, [64, 128], [1, 8]),
            AP(srct, 0, [1, 128], [128, 8]), w=[rCL], slow=True)
    em.do("scalar", lambda e: e.activation(CL[:, 56:64], CL[:, 24:32], AF.Exp),
          r=[rCL], w=[rCL])
    em.do("scalar", lambda e: e.activation(CL[:, 24:32], CL[:, 56:64], AF.Exp, scale=-1.0),
          r=[rCL], w=[rCL])
    em.do("scalar", lambda e: e.activation(CL[:, 32:40], CL[:, 32:40], AF.Exp),
          r=[rCL], w=[rCL])
    dbg_dump("clcols", CL[:], rCL)
    em.do("vector", lambda e: e.memset(CST[:, 0:1], 1.0), w=[rMisc])
    em.do("vector", lambda e: e.memset(CST[:, 1:2], 0.0), w=[rMisc])

    def ln_stats(tt, src_ap, rsrc, scr_ap, rscr):
        p = tt % 2
        q_, s_, t_ = 64 + p, 68 + p, 70 + p
        em.do("scalar", lambda e, s=src_ap, o=scr_ap: e.activation(
            o, s, AF.Square, accum_out=ST[:, q_:q_ + 1]),
            r=[rsrc], w=[rscr, rST])
        em.do("scalar", lambda e, s=src_ap, o=scr_ap: e.activation(
            o, s, AF.Copy, accum_out=ST[:, s_:s_ + 1]),
            r=[rsrc], w=[rscr, rST])
        c = tt
        em.do("vector", lambda e: e.tensor_scalar(MU[:, c:c + 1], ST[:, s_:s_ + 1],
                                                  1.0 / D, None, ALU.mult),
              r=[rST], w=[rMU])
        em.do("vector", lambda e: e.tensor_scalar(ST[:, q_:q_ + 1], ST[:, q_:q_ + 1],
                                                  1.0 / D, None, ALU.mult),
              r=[rST], w=[rST])
        em.do("vector", lambda e: e.tensor_tensor(ST[:, t_:t_ + 1], MU[:, c:c + 1],
                                                  MU[:, c:c + 1], ALU.mult),
              r=[rMU, rST], w=[rST])
        em.do("vector", lambda e: e.tensor_tensor(ST[:, q_:q_ + 1], ST[:, q_:q_ + 1],
                                                  ST[:, t_:t_ + 1], ALU.subtract),
              r=[rST], w=[rST])
        em.do("vector", lambda e: e.tensor_scalar(ST[:, q_:q_ + 1], ST[:, q_:q_ + 1],
                                                  1e-5, None, ALU.add),
              r=[rST], w=[rST])
        em.do("vector", lambda e: e.reciprocal(ST[:, q_:q_ + 1], ST[:, q_:q_ + 1]),
              r=[rST], w=[rST])
        em.do("scalar", lambda e: e.activation(RS[:, c:c + 1], ST[:, q_:q_ + 1], AF.Sqrt),
              r=[rST], w=[rRS])
        em.do("vector", lambda e: e.tensor_tensor(MB[:, c:c + 1], MU[:, c:c + 1],
                                                  RS[:, c:c + 1], ALU.mult),
              r=[rMU, rRS], w=[rMB])
        em.do("vector", lambda e: e.tensor_scalar(MB[:, c:c + 1], MB[:, c:c + 1],
                                                  -1.0, None, ALU.mult),
              r=[rMB], w=[rMB])

    fence0 = []
    rYB0 = rnew(REG_R, "yb0", fence0)
    rP1s = rnew(REG_R, "p1scr", fence0)
    for d in range(ND):
        em.do("vector", lambda e, d=d: e.memset(A[:, 2052 * d:2052 * d + 1], 0.0),
              w=[rA[d]])
    for g in range(4):
        for q in range(4):
            tt = 4 * g + q
            ln_stats(tt, X[:, 1024 * tt:1024 * (tt + 1)], rX[tt], TS[1][:], rP1s)
        for q in range(4):
            tt = 4 * g + q
            em.do("scalar", lambda e, tt=tt: e.activation(
                YB0[:], X[:, 1024 * tt:1024 * (tt + 1)], AF.Identity,
                scale=RS[:, tt:tt + 1], bias=MB[:, tt:tt + 1]),
                r=[rX[tt], rRS, rMB], w=[rYB0])
            for d in range(ND):
                pt, po = bank(d)
                em.do("tensor", lambda e, d=d, q=q, pt=pt, po=po: e.matmul(
                    pt[:, po + 128 * q: po + 128 * (q + 1)],
                    YB0[:, 128 * d:128 * (d + 1)], ident[:],
                    start=(q == 0), stop=(q == 3)),
                    r=[rYB0, rMisc], w=[rB[d]])
        for d in range(ND):
            pt, po = bank(d)
            em.do("scalar", lambda e, d=d, g=g, pt=pt, po=po: e.activation(
                A[:, 2052 * d + 1 + 512 * g: 2052 * d + 1 + 512 * (g + 1)],
                pt[:, po:po + 512], AF.Copy),
                r=[rB[d]], w=[rA[d]])

    dbg_dump("xt0", X[:, 0:1024], rX[0])
    dbg_dump("mucol", MU[:], rMU)
    dbg_dump("rscol", RS[:], rRS)
    dbg_dump("mbcol", MB[:], rMB)
    for d in range(ND):
        dbg_dump(f"ln0cm{d}", A[:, 2052 * d:2052 * d + 2052], rA[d])
    fr = rfence(REG_R)
    REG_R.clear()
    rF = [rnew(REG_R, f"F{i}", fr) for i in range(5)]
    rRW = [rnew(REG_R, f"RW{i}", fr) for i in range(ND)]
    fw = rfence(REG_W)
    REG_W.clear()
    rRKS = [rnew(REG_W, "rks0", fw), rnew(REG_W, "rks1", fw)]
    TMO = {"k": 0, "v": 8, "r": 16}

    for et in range(ND):
        sl = RKS[et % 2]
        rsl = rRKS[et % 2]
        for wi, wn in enumerate(("k", "v", "r")):
            dma("gpsimd", WSTG[:], AP(rW[wn], 1024 * 128 * et, [1024, 128], [1, 1024]),
                w=[rWSTG])
            for b8 in range(ND):
                pt, po = bank(6 + b8 // 4)
                em.do("tensor", lambda e, b8=b8, pt=pt, po=po: e.matmul(
                    pt[:, po + 128 * (b8 % 4): po + 128 * (b8 % 4 + 1)],
                    WSTG[:, 128 * b8:128 * (b8 + 1)], ident[:],
                    start=(b8 % 4 == 0), stop=(b8 % 4 == 3)),
                    r=[rWSTG, rMisc], w=[rB[6 + b8 // 4]])
            tmo = TMO[wn]
            em.do("vector", lambda e, tmo=tmo: e.tensor_scalar(
                CL[:, 56:64], CL[:, tmo:tmo + 8], -1.0, 1.0, ALU.mult, ALU.add),
                r=[rCL], w=[rCL])
            base = wi * 2048
            for b8 in range(ND):
                pt, po = bank(6 + b8 // 4)
                pc = po + 128 * (b8 % 4)
                eng = "scalar" if b8 % 2 == 0 else "vector"
                em.do(eng, lambda e, sl=sl, base=base, b8=b8, pt=pt, pc=pc, tmo=tmo:
                      e.activation(sl[:, base + 128 * b8: base + 128 * (b8 + 1)],
                                   pt[:, pc:pc + 128], AF.Copy,
                                   scale=CL[:, tmo + b8:tmo + b8 + 1])
                      if hasattr(e, "activation") else
                      e.tensor_scalar(sl[:, base + 128 * b8: base + 128 * (b8 + 1)],
                                      pt[:, pc:pc + 128],
                                      CL[:, tmo + b8:tmo + b8 + 1], None, ALU.mult),
                      r=[rB[6 + b8 // 4], rCL], w=[rsl])
                em.do(eng, lambda e, sl=sl, base=base, b8=b8, pt=pt, pc=pc:
                      e.activation(sl[:, base + 1024 + 128 * b8: base + 1024 + 128 * (b8 + 1)],
                                   pt[:, pc:pc + 128], AF.Copy,
                                   scale=CL[:, 56 + b8:56 + b8 + 1])
                      if hasattr(e, "activation") else
                      e.tensor_scalar(sl[:, base + 1024 + 128 * b8: base + 1024 + 128 * (b8 + 1)],
                                      pt[:, pc:pc + 128],
                                      CL[:, 56 + b8:56 + b8 + 1], None, ALU.mult),
                      r=[rB[6 + b8 // 4], rCL], w=[rsl])
        for wi, (Fdst, rFd) in enumerate(((Fb[0], rF[0]), (Fb[1], rF[1]), (Fb[2], rF[2]))):
            base = wi * 2048
            for tc in range(4):
                pt, po = bank(2 * wi + tc % 2)
                rb_ = rB[2 * wi + tc % 2]
                for b8 in range(ND):
                    em.do("tensor", lambda e, sl=sl, base=base, b8=b8, tc=tc, pt=pt, po=po:
                          e.matmul(pt[:, po:po + 512],
                                   sl[:, base + 128 * b8: base + 128 * (b8 + 1)],
                                   A[:, 2052 * b8 + 1 + 512 * tc: 2052 * b8 + 1 + 512 * (tc + 1)],
                                   start=(b8 == 0), stop=False),
                          r=[rsl, rA[b8]], w=[rb_])
                    em.do("tensor", lambda e, sl=sl, base=base, b8=b8, tc=tc, pt=pt, po=po:
                          e.matmul(pt[:, po:po + 512],
                                   sl[:, base + 1024 + 128 * b8: base + 1024 + 128 * (b8 + 1)],
                                   A[:, 2052 * b8 + 512 * tc: 2052 * b8 + 512 * (tc + 1)],
                                   start=False, stop=(b8 == ND - 1)),
                          r=[rsl, rA[b8]], w=[rb_])
                if wi != 2:
                    em.do("vector", lambda e, Fdst=Fdst, tc=tc, pt=pt, po=po:
                          e.tensor_copy(Fdst[:, 512 * tc:512 * (tc + 1)], pt[:, po:po + 512]),
                          r=[rb_], w=[rFd])
                else:
                    em.do("scalar", lambda e, Fdst=Fdst, tc=tc, pt=pt, po=po:
                          e.activation(Fdst[:, 512 * tc:512 * (tc + 1)], pt[:, po:po + 512],
                                       AF.Copy),
                          r=[rb_], w=[rFd])
        if et == 0:
            dbg_dump("kk0", Fb[0][:, 0:2048], rF[0])
            dbg_dump("vv0", Fb[1][:, 0:2048], rF[1])
            dbg_dump("rr0", Fb[2][:, 0:2048], rF[2])
        F0, F1, F2, F3, F4 = Fb
        ewv = AP(CL, 24 + et, [64, 128], [0, 2048])
        euc = CL[:, 32 + et:33 + et]
        em.do("scalar", lambda e: e.activation(F3[:, 0:2048], F0[:, 0:2048], AF.Exp),
              r=[rF[0]], w=[rF[3]])
        em.do("vector", lambda e: e.tensor_tensor(F4[:, 0:2048], F3[:, 0:2048],
                                                  F1[:, 0:2048], ALU.mult),
              r=[rF[3], rF[1]], w=[rF[4]])
        em.do("vector", lambda e: e.memset(F0[:, 0:1], 0.0), w=[rF[0]])
        em.do("vector", lambda e, ewv=ewv: e.tensor_tensor_scan(
            F0[:, 1:2049], ewv, F4[:, 0:2048], 0.0, ALU.mult, ALU.add),
            r=[rF[4], rCL], w=[rF[0]])
        em.do("vector", lambda e: e.memset(F4[:, 0:1], 0.0), w=[rF[4]])
        em.do("vector", lambda e, ewv=ewv: e.tensor_tensor_scan(
            F4[:, 1:2049], ewv, F3[:, 0:2048], 0.0, ALU.mult, ALU.add),
            r=[rF[3], rCL], w=[rF[4]])
        if et == 0:
            dbg_dump("asc0", Fb[0][:, 0:2049], rF[0])
            dbg_dump("bsc0", Fb[4][:, 0:2049], rF[4])
        em.do("vector", lambda e, euc=euc: e.tensor_scalar(
            F3[:, 0:2048], F3[:, 0:2048], euc, None, ALU.mult),
            r=[rF[3], rCL], w=[rF[3]])
        em.do("vector", lambda e: e.tensor_tensor(F1[:, 0:2048], F3[:, 0:2048],
                                                  F1[:, 0:2048], ALU.mult),
              r=[rF[3], rF[1]], w=[rF[1]])
        em.do("vector", lambda e: e.tensor_tensor(F1[:, 0:2048], F1[:, 0:2048],
                                                  F0[:, 0:2048], ALU.add),
              r=[rF[0], rF[1]], w=[rF[1]])
        em.do("vector", lambda e: e.tensor_tensor(F3[:, 0:2048], F4[:, 0:2048],
                                                  F3[:, 0:2048], ALU.add),
              r=[rF[3], rF[4]], w=[rF[3]])
        em.do("vector", lambda e: e.reciprocal(F3[:, 0:2048], F3[:, 0:2048]),
              r=[rF[3]], w=[rF[3]])
        em.do("vector", lambda e: e.tensor_tensor(F1[:, 0:2048], F1[:, 0:2048],
                                                  F3[:, 0:2048], ALU.mult),
              r=[rF[1], rF[3]], w=[rF[1]])
        em.do("vector", lambda e: e.tensor_scalar(F2[:, 0:2048], F2[:, 0:2048],
                                                  -1.0, 88.0, ALU.mult, ALU.min),
              r=[rF[2]], w=[rF[2]])
        em.do("scalar", lambda e: e.activation(F2[:, 0:2048], F2[:, 0:2048], AF.Exp),
              r=[rF[2]], w=[rF[2]])
        em.do("vector", lambda e: e.tensor_scalar(F2[:, 0:2048], F2[:, 0:2048],
                                                  1.0, None, ALU.add),
              r=[rF[2]], w=[rF[2]])
        em.do("vector", lambda e: e.reciprocal(F2[:, 0:2048], F2[:, 0:2048]),
              r=[rF[2]], w=[rF[2]])
        em.do("vector", lambda e, et=et: e.tensor_tensor(
            RWc[:, 2048 * et:2048 * (et + 1)], F1[:, 0:2048], F2[:, 0:2048], ALU.mult),
            r=[rF[1], rF[2]], w=[rRW[et]])
        dbg_dump(f"rwkv_cm{et}", RWc[:, 2048 * et:2048 * (et + 1)], rRW[et])

    fw = rfence(REG_W)
    REG_W.clear()
    rWOT = rnew(REG_W, "wot", fw)
    for m8 in range(ND):
        dma("gpsimd", WSTG[:], AP(rW["o"], 1024 * 128 * m8, [1024, 128], [1, 1024]),
            w=[rWSTG])
        for b8 in range(ND):
            pt, po = bank(6 + b8 // 4)
            em.do("tensor", lambda e, b8=b8, pt=pt, po=po: e.matmul(
                pt[:, po + 128 * (b8 % 4): po + 128 * (b8 % 4 + 1)],
                WSTG[:, 128 * b8:128 * (b8 + 1)], ident[:],
                start=(b8 % 4 == 0), stop=(b8 % 4 == 3)),
                r=[rWSTG, rMisc], w=[rB[6 + b8 // 4]])
        for hb in range(2):
            pt, po = bank(6 + hb)
            em.do("scalar", lambda e, m8=m8, hb=hb, pt=pt, po=po: e.activation(
                AP(WOT, 1024 * 4 * hb + 128 * m8, [8192, 128], [1024, 4], [1, 128]),
                pt[:, po:po + 512], AF.Copy),
                r=[rB[6 + hb]], w=[rWOT])

    frts = retire(rF[0], rF[1], rF[2], rP1s, rYB0)
    rTS = [Res(f"TS{i}", carry=frts) for i in range(4)]
    REG_R.extend(rTS)
    for tt in range(NT):
        ao = TS[tt % 2]; rao = rTS[tt % 2]
        ys = TS[2 + tt % 2]; rys = rTS[2 + tt % 2]
        for ec in range(2):
            pt, po = bank(tt % 2 * 2 + ec)
            rb_ = rB[tt % 2 * 2 + ec]
            for d8 in range(ND):
                em.do("tensor", lambda e, d8=d8, tt=tt, ec=ec, pt=pt, po=po: e.matmul(
                    pt[:, po:po + 512],
                    RWc[:, 2048 * d8 + 128 * tt: 2048 * d8 + 128 * (tt + 1)],
                    WOT[:, 1024 * d8 + 512 * ec: 1024 * d8 + 512 * (ec + 1)],
                    start=(d8 == 0), stop=(d8 == ND - 1)),
                    r=[rRW[d8], rWOT], w=[rb_])
            em.do("vector", lambda e, ec=ec, ao=ao, pt=pt, po=po: e.tensor_copy(
                ao[:, 512 * ec:512 * (ec + 1)], pt[:, po:po + 512]),
                r=[rb_], w=[rao])
        dbg_dump(f"rwkvout{tt}", ao[:], rao)
        ln_stats(tt, ao[:], rao, ys[:], rys)
        em.do("scalar", lambda e, tt=tt, ao=ao, ys=ys: e.activation(
            ys[:], ao[:], AF.Identity, scale=RS[:, tt:tt + 1], bias=MB[:, tt:tt + 1]),
            r=[rao, rRS, rMB], w=[rys])
        em.do("vector", lambda e, tt=tt, ys=ys: e.tensor_tensor(
            X[:, 1024 * tt:1024 * (tt + 1)], X[:, 1024 * tt:1024 * (tt + 1)],
            ys[:], ALU.add),
            r=[rys], w=[rX[tt]])
        rPk = Res(f"park{tt}")
        dma("sync", AP(x1pk, 1024 * 128 * tt, [1024, 128], [1, 1024]),
            X[:, 1024 * tt:1024 * (tt + 1)], r=[rX[tt]], w=[rPk])
        ln_stats(tt, X[:, 1024 * tt:1024 * (tt + 1)], rX[tt], ys[:], rys)
        em.do("scalar", lambda e, tt=tt: e.activation(
            X[:, 1024 * tt:1024 * (tt + 1)], X[:, 1024 * tt:1024 * (tt + 1)],
            AF.Identity, scale=RS[:, tt:tt + 1], bias=MB[:, tt:tt + 1]),
            r=[rRS, rMB, rPk], w=[rX[tt]])
        dbg_dump(f"h0_{tt}", X[:, 1024 * tt:1024 * (tt + 1)], rX[tt])

    rQC = [None] * ND
    rKC = [None] * ND
    rVC = [None] * 10
    rDT = rBASE = rS0 = rS1 = rS2 = rEXM = rPTS = rPV = rPV2 = None
    rHC = [None] * NJ
    rYB = rSQ2 = rAS = rGS = None
    rWQS = rW1S = rW2S = None

    for li in range(n_layers):
        fr = rfence(REG_R)
        REG_R.clear()
        rYB0l = rnew(REG_R, f"ybl{li}", fr)
        rDT = rnew(REG_R, "DTr", fr)
        rBASE = rnew(REG_R, "BASEr", fr)
        fw = rfence(REG_W)
        REG_W.clear()
        rWQS = [rnew(REG_W, "wqs0", fw), rnew(REG_W, "wqs1", fw)]
        rW1S = [rnew(REG_W, f"w1s{i}", fw) for i in range(4)]
        rW2S = [rnew(REG_W, f"w2s{i}", fw) for i in range(2)]

        dma("sync", AP(CL, 40, [64, 128], [1, 8]),
            AP(lngd, D * li, [1, 128], [128, 8]), w=[rCL], slow=True)
        dma("sync", AP(CL, 48, [64, 128], [1, 8]),
            AP(lnbd, D * li, [1, 128], [128, 8]), w=[rCL], slow=True)
        dma("sync", AP(B1A, 0, [22, 128], [1, 22]),
            AP(b1d, 2 * DI * li, [1, 128], [128, 22]), w=[rB1], slow=True)
        dma("sync", AP(B1G, 0, [22, 128], [1, 21]),
            AP(b1d, 2 * DI * li + DI, [1, 128], [128, 21]), w=[rB1], slow=True)
        dma("sync", AP(B1G, 21, [22, 42], [1, 1]),
            AP(b1d, 2 * DI * li + DI + 21 * 128, [1, 42], [1, 1]), w=[rB1], slow=True)
        dma("gpsimd", B2R[:], AP(b2d, D * li, [0, 1], [1, 1024]), w=[rB2])
        dma("sync", QSB[:], AP(qscd, DH * li, [0, 128], [1, 64]), w=[rQS])
        dma("sync", KSB[:], AP(kscd, DH * li, [0, 128], [1, 64]), w=[rQS])
        em.do("vector", lambda e: e.tensor_copy(QSS[:, 0:32], QSB[:, 32:64]), r=[rQS], w=[rQS])
        em.do("vector", lambda e: e.tensor_copy(QSS[:, 32:64], QSB[:, 0:32]), r=[rQS], w=[rQS])
        em.do("vector", lambda e: e.tensor_copy(KSS[:, 0:32], KSB[:, 32:64]), r=[rQS], w=[rQS])
        em.do("vector", lambda e: e.tensor_copy(KSS[:, 32:64], KSB[:, 0:32]), r=[rQS], w=[rQS])
        for nm, rotn, qt in (("AQ", "rotA", QSB), ("BQ", "rotB", QSS),
                             ("CK", "rotC", KSB), ("DK", "rotD", KSS)):
            dma("sync", BASE[:], AP(c_rot[rotn], 0, [64, 128], [64 * 128, 16], [1, 64]),
                r=[rBASE], w=[rBASE])
            em.do("vector", lambda e, nm=nm, qt=qt: e.tensor_tensor(
                AP(DT[nm], 0, [1024, 128], [64, 16], [1, 64]),
                AP(BASE, 0, [1024, 128], [64, 16], [1, 64]),
                AP(qt, 0, [64, 128], [0, 16], [1, 64]), ALU.mult),
                r=[rBASE, rQS], w=[rDT])

        for d in range(ND):
            old = rA[d]
            rA[d] = Res(f"A{d}", carry=retire(old))
        for d in range(4, ND):
            em.do("vector", lambda e, d=d: e.memset(A[:, 2052 * d:2052 * d + 1], 0.0),
                  w=[rA[d]])
        for g in range(4):
            for q in range(4):
                tt = 4 * g + q
                em.do("vector", lambda e, tt=tt: e.tensor_copy(
                    YB0[:], X[:, 1024 * tt:1024 * (tt + 1)]),
                    r=[rX[tt]], w=[rYB0l])
                for d in range(ND):
                    pt, po = bank(d)
                    em.do("tensor", lambda e, d=d, q=q, pt=pt, po=po: e.matmul(
                        pt[:, po + 128 * q: po + 128 * (q + 1)],
                        YB0[:, 128 * d:128 * (d + 1)], ident[:],
                        start=(q == 0), stop=(q == 3)),
                        r=[rYB0l, rMisc], w=[rB[d]])
            for d in range(ND):
                pt, po = bank(d)
                off = 0 if d < 4 else 1
                em.do("scalar", lambda e, d=d, g=g, off=off, pt=pt, po=po: e.activation(
                    A[:, 2052 * d + off + 512 * g: 2052 * d + off + 512 * (g + 1)],
                    pt[:, po:po + 512], AF.Copy),
                    r=[rB[d]], w=[rA[d]])

        fr2 = fr + retire(rYB0l)
        rQC = [rnew(REG_R, f"QC{i}", fr2) for i in range(ND)]
        rKC = [rnew(REG_R, f"KC{i}", fr2) for i in range(ND)]
        rVC = [rnew(REG_R, f"VC{i}", fr2) for i in range(10)]
        rS0 = rnew(REG_R, "S0r", fr2 + retire(rBASE))
        rS1 = rnew(REG_R, "S1r", fr2 + retire(rBASE))
        rS2 = rnew(REG_R, "S2r", fr2)
        rEXM = rnew(REG_R, "EXMr", fr2)
        rPTS = rnew(REG_R, "PTSr", fr2)
        rPV = rnew(REG_R, "PVr", fr2)
        rPV2 = rnew(REG_R, "PV2r", fr2)

        for hf in range(2):
            tts = list(range(8 * hf, 8 * hf + 8))
            if hf == 1:
                for i in range(ND):
                    rQC[i] = Res(f"QC{i}", carry=retire(rQC[i]))
                    REG_R.append(rQC[i])
                em.do("vector", lambda e: e.tensor_copy(
                    AP(KC, 0, [10240, 128], [1280, 8], [1, 256]),
                    AP(KC, 768, [10240, 128], [1280, 8], [1, 256])),
                    r=[rKC[i2] for i2 in range(ND)],
                    w=[rKC[i2] for i2 in range(ND)])
                em.do("vector", lambda e: e.tensor_copy(
                    AP(VC, 0, [10400, 128], [1040, 2], [1, 1040]),
                    AP(VC, 1040 * 6, [10400, 128], [1040, 2], [1, 1040])),
                    r=[rVC[6], rVC[7]], w=[rVC[0], rVC[1]])
                nsl = range(2, 10)
            else:
                nsl = range(0, 8)
            for i in nsl:
                em.do("vector", lambda e, i=i: e.memset(
                    AP(VC, 1040 * i + 64, [10400, 128], [65, 16], [1, 1]), 1.0),
                    w=[rVC[i]])

            for c in range(6):
                sl = WQS[c % 2]; rsl = rWQS[c % 2]
                dma("gpsimd", sl[:],
                    AP(Wqkvd, 3 * D * D * li + 512 * c,
                       [3 * D, 128], [3 * D * 128, 8], [1, 512]),
                    w=[rsl])
                for q, tt in enumerate(tts):
                    pt, po = bank(q % 4)
                    rb_ = rB[q % 4]
                    for d8 in range(ND):
                        em.do("tensor", lambda e, d8=d8, tt=tt, sl=sl, pt=pt, po=po: e.matmul(
                            pt[:, po:po + 512],
                            A[:, 2052 * d8 + 128 * tt: 2052 * d8 + 128 * (tt + 1)],
                            sl[:, 512 * d8:512 * (d8 + 1)],
                            start=(d8 == 0), stop=(d8 == ND - 1)),
                            r=[rA[d8], rsl], w=[rb_])
                    if c >= 4:
                        vi = tt - 6 * hf
                        em.do("scalar", lambda e, vi=vi, c=c, pt=pt, po=po: e.activation(
                            AP(VC, 1040 * vi + 65 * 8 * (c - 4),
                               [10400, 128], [65, 8], [1, 64]),
                            pt[:, po:po + 512], AF.Copy),
                            r=[rb_], w=[rVC[vi]])
                    else:
                        isq = c < 2
                        TB1 = DT["AQ"] if isq else DT["CK"]
                        TB2 = DT["BQ"] if isq else DT["DK"]
                        em.do("scalar", lambda e, pt=pt, po=po: e.activation(
                            S0[:], pt[:, po:po + 512], AF.Copy),
                            r=[rb_], w=[rS0])
                        em.do("scalar", lambda e: e.activation(S1[:], S0[:], AF.Square),
                              r=[rS0], w=[rS1])
                        em.do("vector", lambda e: e.tensor_reduce(
                            SSQ[:, 0:8], AP(S1, 0, [512, 128], [64, 8], [1, 64]),
                            AXX, ALU.add),
                            r=[rS1], w=[rSSQ])
                        em.do("vector", lambda e: e.tensor_scalar(
                            SSQ[:, 0:8], SSQ[:, 0:8], 1e-24, None, ALU.max),
                            r=[rSSQ], w=[rSSQ])
                        em.do("vector", lambda e: e.reciprocal(SSQ[:, 0:8], SSQ[:, 0:8]),
                              r=[rSSQ], w=[rSSQ])
                        em.do("scalar", lambda e: e.activation(SSQ[:, 0:8], SSQ[:, 0:8],
                                                               AF.Sqrt),
                              r=[rSSQ], w=[rSSQ])
                        tto = 64 * tt
                        em.do("vector", lambda e, TB1=TB1, tto=tto: e.tensor_tensor(
                            S1[:], S0[:],
                            AP(TB1, tto, [1024, 128], [0, 8], [1, 64]), ALU.mult),
                            r=[rS0, rDT], w=[rS1])
                        em.do("vector", lambda e, TB2=TB2, tto=tto: e.tensor_tensor(
                            AP(S2, 0, [512, 128], [64, 8], [1, 32]),
                            AP(S0, 32, [512, 128], [64, 8], [1, 32]),
                            AP(TB2, tto, [1024, 128], [0, 8], [1, 32]), ALU.mult),
                            r=[rS0, rDT], w=[rS2])
                        em.do("vector", lambda e, TB2=TB2, tto=tto: e.tensor_tensor(
                            AP(S2, 32, [512, 128], [64, 8], [1, 32]),
                            AP(S0, 0, [512, 128], [64, 8], [1, 32]),
                            AP(TB2, tto + 32, [1024, 128], [0, 8], [1, 32]), ALU.mult),
                            r=[rS0, rDT], w=[rS2])
                        em.do("vector", lambda e: e.tensor_tensor(S1[:], S1[:], S2[:],
                                                                  ALU.add),
                              r=[rS1, rS2], w=[rS1])
                        em.do("vector", lambda e: e.tensor_tensor(
                            EXM[:, 0:512], S1[:],
                            AP(SSQ, 0, [8, 128], [1, 8], [0, 64]), ALU.mult),
                            r=[rS1, rSSQ], w=[rEXM])
                        pt2, po2 = bank(4 + q % 2)
                        rb2 = rB[4 + q % 2]
                        for b4 in range(4):
                            em.do("tensor", lambda e, b4=b4, pt2=pt2, po2=po2: e.matmul(
                                pt2[:, po2 + 128 * b4: po2 + 128 * (b4 + 1)],
                                EXM[:, 128 * b4:128 * (b4 + 1)], ident[:],
                                start=(b4 == 0), stop=(b4 == 3)),
                                r=[rEXM, rMisc], w=[rb2])
                        if isq:
                            dst, wdt, colo = QC, 1024, 128 * (tt - 8 * hf)
                            rowo = 1024 * 4 * c
                            rws = [rQC[4 * c + b4] for b4 in range(4)]
                        else:
                            dst, wdt, colo = KC, 1280, 128 * tt - 768 * hf
                            rowo = 1280 * 4 * (c - 2)
                            rws = [rKC[4 * (c - 2) + b4] for b4 in range(4)]
                        em.do("scalar", lambda e, dst=dst, rowo=rowo, colo=colo, wdt=wdt,
                              pt2=pt2, po2=po2: e.activation(
                            AP(dst, rowo + colo, [dst.shape[1], 128], [wdt, 4], [1, 128]),
                            pt2[:, po2:po2 + 512], AF.Copy),
                            r=[rb2], w=rws)

            for itl in range(8):
                it = 8 * hf + itl
                Wd = 128 if it == 0 else (256 if it == 1 else 384)
                nbl = Wd // 128
                lo = 0 if it < 2 else 128 * (it - 2) - 768 * hf
                vbase = 0 if it < 2 else (it - 2) - 6 * hf
                moff = 384 - Wd
                pt_sim = PS[itl % 2]
                simres = rB[0 + 2 * (itl % 2)]
                simres2 = rB[1 + 2 * (itl % 2)]
                for hp in range(ND):
                    em.do("tensor", lambda e, hp=hp, itl=itl, lo=lo, Wd=Wd, pt=pt_sim:
                          e.matmul(pt[0:128, 0:Wd],
                                   QC[0:64, 1024 * hp + 128 * itl: 1024 * hp + 128 * (itl + 1)],
                                   KC[0:64, 1280 * hp + lo: 1280 * hp + lo + Wd],
                                   start=True, stop=True, tile_position=(0, 0)),
                          r=[rQC[hp], rKC[hp]], w=[simres])
                    em.do("tensor", lambda e, hp=hp, itl=itl, lo=lo, Wd=Wd, pt=pt_sim:
                          e.matmul(pt[0:128, 512:512 + Wd],
                                   QC[64:128, 1024 * hp + 128 * itl: 1024 * hp + 128 * (itl + 1)],
                                   KC[64:128, 1280 * hp + lo: 1280 * hp + lo + Wd],
                                   start=True, stop=True, tile_position=(64, 0)),
                          r=[rQC[hp], rKC[hp]], w=[simres2])
                    em.do("vector", lambda e, Wd=Wd, pt=pt_sim: e.tensor_reduce(
                        NMX[:, 0:2], AP(pt, 0, [1024, 128], [512, 2], [1, Wd]),
                        AXX, ALU.max, negate=True),
                        r=[simres, simres2], w=[rNMX])
                    em.do("scalar", lambda e, Wd=Wd, pt=pt_sim: e.activation(
                        EXM[:, 0:Wd], pt[:, 0:Wd], AF.Exp, bias=NMX[:, 0:1]),
                        r=[simres, rNMX], w=[rEXM])
                    em.do("scalar", lambda e, Wd=Wd, pt=pt_sim: e.activation(
                        EXM[:, 384:384 + Wd], pt[:, 512:512 + Wd], AF.Exp,
                        bias=NMX[:, 1:2]),
                        r=[simres2, rNMX], w=[rEXM])
                    em.do("vector", lambda e, Wd=Wd, moff=moff: e.tensor_tensor(
                        AP(EXM, 0, [768, 128], [384, 2], [1, Wd]),
                        AP(EXM, 0, [768, 128], [384, 2], [1, Wd]),
                        AP(maskS, moff, [384, 128], [0, 2], [1, Wd]), ALU.mult),
                        r=[rMisc], w=[rEXM])
                    pt3 = PS[2]
                    rb3 = rB[4]; rb4 = rB[5]
                    for b in range(nbl):
                        em.do("tensor", lambda e, b=b, nbl=nbl, pt3=pt3: e.matmul(
                            pt3[:, 128 * b:128 * (b + 1)],
                            EXM[:, 128 * b:128 * (b + 1)], ident[:],
                            start=(b == 0), stop=(b == nbl - 1)),
                            r=[rEXM, rMisc], w=[rb3])
                        em.do("tensor", lambda e, b=b, nbl=nbl, pt3=pt3: e.matmul(
                            pt3[:, 512 + 128 * b:512 + 128 * (b + 1)],
                            EXM[:, 384 + 128 * b:384 + 128 * (b + 1)], ident[:],
                            start=(b == 0), stop=(b == nbl - 1)),
                            r=[rEXM, rMisc], w=[rb4])
                    em.do("vector", lambda e, Wd=Wd, pt3=pt3: e.tensor_copy(
                        AP(PTS, 0, [768, 128], [384, 2], [1, Wd]),
                        AP(pt3, 0, [1024, 128], [512, 2], [1, Wd])),
                        r=[rb3, rb4], w=[rPTS])
                    pt4 = PS[3]
                    rb5 = rB[6]; rb6 = rB[7]
                    for b in range(nbl):
                        vs = vbase + b
                        em.do("tensor", lambda e, b=b, vs=vs, hp=hp, nbl=nbl, pt4=pt4:
                              e.matmul(pt4[:, 0:65],
                                       PTS[:, 128 * b:128 * (b + 1)],
                                       VC[:, 1040 * vs + 65 * 2 * hp: 1040 * vs + 65 * (2 * hp + 1)],
                                       start=(b == 0), stop=(b == nbl - 1)),
                              r=[rPTS, rVC[vs]], w=[rb5])
                        em.do("tensor", lambda e, b=b, vs=vs, hp=hp, nbl=nbl, pt4=pt4:
                              e.matmul(pt4[:, 512:577],
                                       PTS[:, 384 + 128 * b:384 + 128 * (b + 1)],
                                       VC[:, 1040 * vs + 65 * (2 * hp + 1): 1040 * vs + 65 * (2 * hp + 2)],
                                       start=(b == 0), stop=(b == nbl - 1)),
                              r=[rPTS, rVC[vs]], w=[rb6])
                    em.do("vector", lambda e, pt4=pt4: e.reciprocal(
                        RPC[:, 0:2], AP(pt4, 64, [1024, 128], [512, 2], [1, 1])),
                        r=[rb5, rb6], w=[rRPC])
                    em.do("scalar", lambda e, pt4=pt4: e.activation(
                        PVo[:], pt4[:, 0:64], AF.Copy, scale=RPC[:, 0:1]),
                        r=[rb5, rRPC], w=[rPV])
                    em.do("scalar", lambda e, pt4=pt4: e.activation(
                        PVo2[:], pt4[:, 512:576], AF.Copy, scale=RPC[:, 1:2]),
                        r=[rb6, rRPC], w=[rPV2])
                    em.do("tensor", lambda e, pt4=pt4: e.matmul(
                        pt4[0:64, 128:256], PVo[:], ident[:], start=True, stop=True),
                        r=[rPV, rMisc], w=[rb5])
                    em.do("tensor", lambda e, pt4=pt4: e.matmul(
                        pt4[64:128, 128:256], PVo2[:], ident[:], start=True, stop=True,
                        tile_position=(0, 64)),
                        r=[rPV2, rMisc], w=[rb5])
                    em.do("vector", lambda e, hp=hp, itl=itl, pt4=pt4: e.tensor_copy(
                        QC[:, 1024 * hp + 128 * itl: 1024 * hp + 128 * (itl + 1)],
                        pt4[:, 128:256]),
                        r=[rb5], w=[rQC[hp]])

            for ec in range(2):
                sl = WQS[ec]; rsl = rWQS[ec]
                dma("gpsimd", sl[:],
                    AP(iWod, D * D * li + 512 * ec, [D, 128], [D * 128, 8], [1, 512]),
                    w=[rsl])
                for q, tt in enumerate(tts):
                    pt, po = bank(q % 4)
                    rb_ = rB[q % 4]
                    for hp in range(ND):
                        em.do("tensor", lambda e, hp=hp, tt=tt, hf=hf, sl=sl, pt=pt, po=po:
                              e.matmul(pt[:, po:po + 512],
                                       QC[:, 1024 * hp + 128 * (tt - 8 * hf):
                                          1024 * hp + 128 * (tt - 8 * hf + 1)],
                                       sl[:, 512 * hp:512 * (hp + 1)],
                                       start=(hp == 0), stop=(hp == ND - 1)),
                              r=[rQC[hp], rsl], w=[rb_])
                    em.do("vector", lambda e, tt=tt, ec=ec, pt=pt, po=po: e.tensor_tensor(
                        X[:, 1024 * tt + 512 * ec: 1024 * tt + 512 * (ec + 1)],
                        X[:, 1024 * tt + 512 * ec: 1024 * tt + 512 * (ec + 1)],
                        pt[:, po:po + 512], ALU.add),
                        r=[rb_], w=[rX[tt]])
            for tt in tts:
                dbg_dump(f"L{li}attn{tt}", X[:, 1024 * tt:1024 * (tt + 1)], rX[tt])

        fr = rfence(REG_R)
        REG_R.clear()
        rHC = [rnew(REG_R, f"HC{i}", fr) for i in range(NJ)]
        rYB = rnew(REG_R, "YBr", fr)
        rSQ2 = rnew(REG_R, "SQ2", fr)
        rAS = rnew(REG_R, "ASr", fr)
        rGS = rnew(REG_R, "GSr", fr)
        fw = rfence(REG_W)
        REG_W.clear()
        rWQS = [rnew(REG_W, "wqs0b", fw), rnew(REG_W, "wqs1b", fw)]
        rW1S = [rnew(REG_W, f"w1sb{i}", fw) for i in range(4)]
        rW2S = [rnew(REG_W, f"w2sb{i}", fw) for i in range(2)]

        for tt in range(NT):
            em.do("scalar", lambda e, tt=tt: e.activation(
                SQs[:], X[:, 1024 * tt:1024 * (tt + 1)], AF.Square),
                r=[rX[tt]], w=[rSQ2])
            em.do("vector", lambda e, tt=tt: e.tensor_reduce(
                ST[:, tt:tt + 1], X[:, 1024 * tt:1024 * tt + 512], AXX, ALU.add),
                r=[rX[tt]], w=[rST])
            em.do("vector", lambda e, tt=tt: e.tensor_reduce(
                ST[:, 32 + tt:33 + tt], X[:, 1024 * tt + 512:1024 * (tt + 1)],
                AXX, ALU.add),
                r=[rX[tt]], w=[rST])
            em.do("vector", lambda e, tt=tt: e.tensor_reduce(
                ST[:, 16 + tt:17 + tt], SQs[:, 0:512], AXX, ALU.add),
                r=[rSQ2], w=[rST])
            em.do("vector", lambda e, tt=tt: e.tensor_reduce(
                ST[:, 48 + tt:49 + tt], SQs[:, 512:1024], AXX, ALU.add),
                r=[rSQ2], w=[rST])
        dma("sync", SHP[1:128, 0:32], ST[0:127, 32:64], r=[rST], w=[rSHP])
        dma("sync", SHP[0:1, 1:16], ST[127:128, 32:47], r=[rST], w=[rSHP])
        dma("sync", SHP[0:1, 17:32], ST[127:128, 48:63], r=[rST], w=[rSHP])
        em.do("vector", lambda e: e.memset(SHP[0:1, 0:1], 0.0), w=[rSHP])
        em.do("vector", lambda e: e.memset(SHP[0:1, 16:17], 0.0), w=[rSHP])
        em.do("vector", lambda e: e.tensor_tensor(MU[:], ST[:, 0:16], SHP[:, 0:16],
                                                  ALU.add), r=[rST, rSHP], w=[rMU])
        em.do("vector", lambda e: e.tensor_scalar(MU[:], MU[:], 1.0 / D, None, ALU.mult),
              r=[rMU], w=[rMU])
        em.do("vector", lambda e: e.tensor_tensor(ST[:, 64:80], ST[:, 16:32],
                                                  SHP[:, 16:32], ALU.add),
              r=[rST, rSHP], w=[rST])
        em.do("vector", lambda e: e.tensor_scalar(ST[:, 64:80], ST[:, 64:80], 1.0 / D,
                                                  None, ALU.mult), r=[rST], w=[rST])
        em.do("vector", lambda e: e.tensor_tensor(ST[:, 80:96], MU[:], MU[:], ALU.mult),
              r=[rMU], w=[rST])
        em.do("vector", lambda e: e.tensor_tensor(ST[:, 64:80], ST[:, 64:80],
                                                  ST[:, 80:96], ALU.subtract),
              r=[rST], w=[rST])
        em.do("vector", lambda e: e.tensor_scalar(ST[:, 64:80], ST[:, 64:80], 1e-5,
                                                  None, ALU.add), r=[rST], w=[rST])
        em.do("vector", lambda e: e.reciprocal(ST[:, 64:80], ST[:, 64:80]),
              r=[rST], w=[rST])
        em.do("scalar", lambda e: e.activation(RS[:], ST[:, 64:80], AF.Sqrt),
              r=[rST], w=[rRS])
        em.do("vector", lambda e: e.tensor_tensor(MB[:], MU[:], RS[:], ALU.mult),
              r=[rMU, rRS], w=[rMB])
        em.do("vector", lambda e: e.tensor_scalar(MB[:], MB[:], -1.0, None, ALU.mult),
              r=[rMB], w=[rMB])
        dma("sync", RSp[0:127, :], RS[1:128, :], r=[rRS], w=[rRSp])
        dma("sync", RSp[127:128, 0:15], RS[0:1, 1:16], r=[rRS], w=[rRSp])
        dma("sync", MBp[0:127, :], MB[1:128, :], r=[rMB], w=[rMBp])
        dma("sync", MBp[127:128, 0:15], MB[0:1, 1:16], r=[rMB], w=[rMBp])
        dma("sync", RSp[127:128, 15:16], CST[0:1, 0:1], r=[rMisc], w=[rRSp])
        dma("sync", MBp[127:128, 15:16], CST[0:1, 1:2], r=[rMisc], w=[rMBp])
        for d in range(ND):
            old = rA[d]
            rA[d] = Res(f"A{d}", carry=retire(old))
        for g in range(4):
            for q in range(4):
                tt = 4 * g + q
                em.do("scalar", lambda e, tt=tt: e.activation(
                    YB[:, 0:512], X[:, 1024 * tt:1024 * tt + 512], AF.Identity,
                    scale=RS[:, tt:tt + 1], bias=MB[:, tt:tt + 1]),
                    r=[rX[tt], rRS, rMB], w=[rYB])
                em.do("scalar", lambda e, tt=tt: e.activation(
                    YB[:, 512:1024], X[:, 1024 * tt + 512:1024 * (tt + 1)], AF.Identity,
                    scale=RSp[:, tt:tt + 1], bias=MBp[:, tt:tt + 1]),
                    r=[rX[tt], rRSp, rMBp], w=[rYB])
                for d in range(ND):
                    pt, po = bank(d)
                    em.do("tensor", lambda e, d=d, q=q, pt=pt, po=po: e.matmul(
                        pt[:, po + 128 * q: po + 128 * (q + 1)],
                        YB[:, 128 * d:128 * (d + 1)], ident[:],
                        start=(q == 0), stop=(q == 3)),
                        r=[rYB, rMisc], w=[rB[d]])
            for d in range(ND):
                pt, po = bank(d)
                off = 0 if d < 4 else 1
                em.do("scalar", lambda e, d=d, g=g, off=off, pt=pt, po=po: e.activation(
                    A[:, 2052 * d + off + 512 * g: 2052 * d + off + 512 * (g + 1)],
                    pt[:, po:po + 512], AF.Identity,
                    scale=CL[:, 40 + d:41 + d], bias=CL[:, 48 + d:49 + d]),
                    r=[rB[d], rCL], w=[rA[d]])
        em.do("vector", lambda e: e.tensor_copy(MB1[0:1, 0:1], MB[0:1, 0:1]),
              r=[rMB], w=[rMB1])
        pt6, po6 = bank(0)
        em.do("tensor", lambda e, pt6=pt6: e.matmul(
            pt6[:, 0:1], onesS[:], MB1[0:1, 0:1], start=True, stop=True),
            r=[rMB1, rMisc], w=[rB[0]])
        for d in range(4, ND):
            em.do("scalar", lambda e, d=d, pt6=pt6: e.activation(
                A[:, 2052 * d:2052 * d + 1], pt6[:, 0:1], AF.Identity,
                scale=CL[:, 40 + d:41 + d], bias=CL[:, 48 + d:49 + d]),
                r=[rB[0], rCL], w=[rA[d]])

        for tch in range(2):
            for jt in range(NJ):
                wA = 128 if jt < NJ - 1 else 42
                slA = W1S[jt % 2]; rslA = rW1S[jt % 2]
                slG = W1S[2 + jt % 2]; rslG = rW1S[2 + jt % 2]
                dma("gpsimd", slA[:, 0:8 * wA],
                    AP(W1d, 2 * DI * D * li + 128 * jt,
                       [2 * DI, 128], [2 * DI * 128, 8], [1, wA]),
                    w=[rslA])
                dma("gpsimd", slG[:, 0:8 * wA],
                    AP(W1d, 2 * DI * D * li + DI + 128 * jt,
                       [2 * DI, 128], [2 * DI * 128, 8], [1, wA]),
                    w=[rslG])
                for sc in range(2):
                    ptA, poA = bank(4 + (jt % 2))
                    ptG, poG = bank(6 + (jt % 2))
                    rbA = rB[4 + (jt % 2)]; rbG = rB[6 + (jt % 2)]
                    for d8 in range(ND):
                        em.do("tensor", lambda e, d8=d8, wA=wA, tch=tch, sc=sc, slA=slA,
                              ptA=ptA, poA=poA: e.matmul(
                            ptA[0:wA, poA:poA + 512],
                            slA[:, wA * d8:wA * (d8 + 1)],
                            A[:, 2052 * d8 + 1024 * tch + 512 * sc:
                               2052 * d8 + 1024 * tch + 512 * (sc + 1)],
                            start=(d8 == 0), stop=(d8 == ND - 1)),
                            r=[rslA, rA[d8]], w=[rbA])
                        em.do("tensor", lambda e, d8=d8, wA=wA, tch=tch, sc=sc, slG=slG,
                              ptG=ptG, poG=poG: e.matmul(
                            ptG[0:wA, poG:poG + 512],
                            slG[:, wA * d8:wA * (d8 + 1)],
                            A[:, 2052 * d8 + 1024 * tch + 512 * sc:
                               2052 * d8 + 1024 * tch + 512 * (sc + 1)],
                            start=(d8 == 0), stop=(d8 == ND - 1)),
                            r=[rslG, rA[d8]], w=[rbG])
                    em.do("scalar", lambda e, wA=wA, jt=jt, ptG=ptG, poG=poG: e.activation(
                        GSb[0:wA, :], ptG[0:wA, poG:poG + 512], AF.Gelu,
                        bias=B1G[0:wA, jt:jt + 1]),
                        r=[rbG, rB1], w=[rGS])
                    em.do("vector", lambda e, wA=wA, jt=jt, ptA=ptA, poA=poA: e.tensor_scalar(
                        ASb[0:wA, :], ptA[0:wA, poA:poA + 512],
                        B1A[0:wA, jt:jt + 1], None, ALU.add),
                        r=[rbA, rB1], w=[rAS])
                    em.do("vector", lambda e, wA=wA, jt=jt, sc=sc: e.tensor_tensor(
                        HC[0:wA, 1024 * jt + 512 * sc:1024 * jt + 512 * (sc + 1)],
                        ASb[0:wA, :], GSb[0:wA, :], ALU.mult),
                        r=[rAS, rGS], w=[rHC[jt]])
            for ttg in range(2):
                for kt in range(NJ):
                    kw = 128 if kt < NJ - 1 else 42
                    sl = W2S[kt % 2]; rsl = rW2S[kt % 2]
                    dma("gpsimd", sl[0:kw, :],
                        AP(W2d, DI * D * li + 128 * kt * D, [D, kw], [1, 1024]),
                        w=[rsl])
                    for tq in range(4):
                        twc = ttg * 4 + tq
                        for ec in range(2):
                            pt, po = bank(tq * 2 + ec)
                            rb_ = rB[tq * 2 + ec]
                            em.do("tensor", lambda e, kt=kt, kw=kw, twc=twc, ec=ec, sl=sl,
                                  pt=pt, po=po: e.matmul(
                                pt[:, po:po + 512],
                                HC[0:kw, 1024 * kt + 128 * twc:1024 * kt + 128 * (twc + 1)],
                                sl[0:kw, 512 * ec:512 * (ec + 1)],
                                start=(kt == 0), stop=False),
                                r=[rsl, rHC[kt]], w=[rb_])
                for tq in range(4):
                    twc = ttg * 4 + tq
                    tt = tch * 8 + twc
                    for ec in range(2):
                        pt, po = bank(tq * 2 + ec)
                        rb_ = rB[tq * 2 + ec]
                        em.do("tensor", lambda e, ec=ec, pt=pt, po=po: e.matmul(
                            pt[:, po:po + 512], onesS[:],
                            B2R[0:1, 512 * ec:512 * (ec + 1)],
                            start=False, stop=True),
                            r=[rB2, rMisc], w=[rb_])
                        em.do("vector", lambda e, tt=tt, ec=ec, pt=pt, po=po: e.tensor_tensor(
                            X[:, 1024 * tt + 512 * ec:1024 * tt + 512 * (ec + 1)],
                            X[:, 1024 * tt + 512 * ec:1024 * tt + 512 * (ec + 1)],
                            pt[:, po:po + 512], ALU.add),
                            r=[rb_], w=[rX[tt]])
        for tt in range(NT):
            dbg_dump(f"L{li}end{tt}", X[:, 1024 * tt:1024 * (tt + 1)], rX[tt])

    fr = rfence(REG_R)
    REG_R.clear()
    rTl = [rnew(REG_R, f"tl{i}", fr) for i in range(4)]
    rSQt = rnew(REG_R, "sqt", fr)
    for tt in range(NT):
        xs = TS[tt % 2]; rxs = rTl[tt % 2]
        ys = TS[2 + tt % 2]; rys = rTl[2 + tt % 2]
        dma("sync", xs[:], AP(x1pk, 1024 * 128 * tt, [1024, 128], [1, 1024]),
            r=[], w=[rxs])
        ln_stats(tt, X[:, 1024 * tt:1024 * (tt + 1)], rX[tt], SQs[:], rSQt)
        em.do("scalar", lambda e, tt=tt, ys=ys: e.activation(
            ys[:], X[:, 1024 * tt:1024 * (tt + 1)], AF.Identity,
            scale=RS[:, tt:tt + 1], bias=MB[:, tt:tt + 1]),
            r=[rX[tt], rRS, rMB], w=[rys])
        em.do("vector", lambda e, ys=ys, xs=xs: e.tensor_tensor(
            ys[:], ys[:], xs[:], ALU.add),
            r=[rxs], w=[rys])
        h = dma("sync", AP(outd, 1024 * 128 * tt, [1024, 128], [1, 1024]), ys[:],
                r=[rys], w=[rys])
        store_h.append(h)

    em.finalize(nc, store_h)
    return nc


_CACHE = {}


def _pack_inputs(L0_attn_x, L0_attn_time_decay, L0_attn_time_first, L0_attn_time_mix_k,
                 L0_attn_time_mix_v, L0_attn_time_mix_r, L0_attn_Wk, L0_attn_Wv,
                 L0_attn_Wr, L0_attn_Wo, L0_ffn_Wqkv, L0_ffn_q_scale, L0_ffn_k_scale,
                 L0_ffn_Wo, L0_ffn_ln_g, L0_ffn_ln_b, L0_ffn_W1, L0_ffn_b1,
                 L0_ffn_W2, L0_ffn_b2):
    hc = host_consts()
    base = dict(
        td=np.ascontiguousarray(L0_attn_time_decay, dtype=np.float32),
        tf=np.ascontiguousarray(L0_attn_time_first, dtype=np.float32),
        tmk=np.ascontiguousarray(L0_attn_time_mix_k, dtype=np.float32).reshape(D),
        tmv=np.ascontiguousarray(L0_attn_time_mix_v, dtype=np.float32).reshape(D),
        tmr=np.ascontiguousarray(L0_attn_time_mix_r, dtype=np.float32).reshape(D),
        rWk=np.ascontiguousarray(L0_attn_Wk, dtype=np.float32),
        rWv=np.ascontiguousarray(L0_attn_Wv, dtype=np.float32),
        rWr=np.ascontiguousarray(L0_attn_Wr, dtype=np.float32),
        rWo=np.ascontiguousarray(L0_attn_Wo, dtype=np.float32),
        Wqkv=np.ascontiguousarray(L0_ffn_Wqkv, dtype=np.float32),
        qsc=np.ascontiguousarray(L0_ffn_q_scale, dtype=np.float32),
        ksc=np.ascontiguousarray(L0_ffn_k_scale, dtype=np.float32),
        iWo=np.ascontiguousarray(L0_ffn_Wo, dtype=np.float32),
        lng=np.ascontiguousarray(L0_ffn_ln_g, dtype=np.float32),
        lnb=np.ascontiguousarray(L0_ffn_ln_b, dtype=np.float32),
        W1=np.ascontiguousarray(L0_ffn_W1, dtype=np.float32),
        b1=np.ascontiguousarray(L0_ffn_b1, dtype=np.float32),
        W2=np.ascontiguousarray(L0_ffn_W2, dtype=np.float32),
        b2=np.ascontiguousarray(L0_ffn_b2, dtype=np.float32),
        **hc,
    )
    x = np.ascontiguousarray(L0_attn_x, dtype=np.float32)
    return [dict(base, x0=np.ascontiguousarray(x[b])) for b in range(8)]


def kernel(**inputs):
    if "nc" not in _CACHE:
        _CACHE["nc"] = build_program()
    nc = _CACHE["nc"]
    in_maps = _pack_inputs(**inputs)
    res = run_bass_kernel_spmd(nc, in_maps, core_ids=list(range(8)))
    return np.stack([res.results[b]["out"] for b in range(8)], axis=0)
